# version 68
# baseline (speedup 1.0000x reference)
import numpy as np
import ml_dtypes
from contextlib import ExitStack

import concourse.bass as bass
import concourse.mybir as mybir
from concourse.bass_utils import run_bass_kernel_spmd

F32 = mybir.dt.float32
BF16 = mybir.dt.bfloat16
U8 = mybir.dt.uint8
AF = mybir.ActivationFunctionType
ALU = mybir.AluOpType
AX = mybir.AxisListType

E = 1024
H = 16
DH = 64
LAT = 512
PAGE = 256
NE = E // 128
NLAT = LAT // 128
SCALE = 0.125
EPS = 1e-5
N_CORES = 8
KVER = 61

ENGS = ("PE", "ACT", "DVE", "POOL", "SP")


class Arena:

    def __init__(self, nc, size):
        self.nc = nc
        self.size = size
        self.free_list = [(0, size)]
        self.base = None
        self.live = {}
        self.n = 0
        self.high = 0

    def attach(self, stack):
        self.backing = stack.enter_context(
            self.nc.sbuf_tensor("arena_backing", [128, self.size], U8)
        )
        self.base = self.nc.lookup_mloc(self.backing).addr

    def alloc(self, name, shape, dtype):
        per = 1
        for s in shape[1:]:
            per *= s
        per = (per * mybir.dt.size(dtype) + 31) & ~31
        for i, (off, sz) in enumerate(self.free_list):
            if sz >= per:
                if sz == per:
                    self.free_list.pop(i)
                else:
                    self.free_list[i] = (off + per, sz - per)
                break
        else:
            raise MemoryError(f"SBUF arena OOM for {name} ({per}B); free={self.free_list}")
        self.n += 1
        uname = f"{name}_{self.n}"
        h = self.nc.virtual_sbuf_tensor(uname, shape, dtype)
        self.nc._tensor(h.name, h.shape, h.dtype, type="SB")
        mloc = self.nc.lookup_mloc(h)
        mloc.allocated = True
        mloc.addr = self.base + off
        self.live[h.name] = (off, per)
        self.high = max(self.high, off + per)
        return h

    def region(self, h):
        off, per = self.live[h.name]
        return off, off + per

    def free(self, h):
        off, per = self.live.pop(h.name)
        self.free_list.append((off, per))
        self.free_list.sort()
        merged = []
        for o, s in self.free_list:
            if merged and merged[-1][0] + merged[-1][1] == o:
                merged[-1] = (merged[-1][0], merged[-1][1] + s)
            else:
                merged.append((o, s))
        self.free_list = [tuple(x) for x in merged]
        return off, off + per


class Prog:

    def __init__(self, nc, stack):
        self.nc = nc
        self.ops = {e: [] for e in ENGS}
        self.sems = {}
        self.ctr = {}
        for e in ENGS:
            self.sems["s" + e] = stack.enter_context(nc.semaphore("s" + e))
            self.ctr["s" + e] = 0
        self.dma_sems = {}
        self.dma_issued = {}
        self.floors = {e: {} for e in ENGS}
        self.sem_events = {}
        self.state = {}
        self.pending = {e: [] for e in ENGS}
        self.n_waits = 0
        self.freed_clock = []

    def add_dma_sem(self, name, stack):
        self.dma_sems[name] = stack.enter_context(self.nc.semaphore(name))
        self.dma_issued[name] = 0

    @staticmethod
    def _merge(dst, src):
        if not src:
            return
        for s, v in src.items():
            if dst.get(s, 0) < v:
                dst[s] = v

    def _do_waits(self, eng_name, need):
        fl = self.floors[eng_name]
        for s in sorted(need, key=str):
            v = need[s]
            if s in self.dma_issued:
                v = max(v, self.dma_issued[s])
            if fl.get(s, 0) >= v:
                continue
            h = self.sems.get(s) or self.dma_sems[s]
            self.ops[eng_name].append(lambda e, h=h, v=v: e.wait_ge(h, v))
            self.n_waits += 1
            fl[s] = v
            ev = self.sem_events.get((s, v))
            if ev:
                self._merge(fl, ev)

    def _needs(self, reads, writes, pe_accum=False, eng_name=None):
        need = {}
        for k in reads:
            st = self.state.get(k)
            if st and isinstance(st["w"], dict):
                self._merge(need, st["w"])
        for k in writes:
            st = self.state.get(k)
            if st:
                w = st["w"]
                if isinstance(w, dict):
                    if not (pe_accum and eng_name == "PE" and set(w) == {"sPE"}):
                        self._merge(need, w)
                for rc in st["r"]:
                    if isinstance(rc, dict):
                        self._merge(need, rc)
        return need

    def _resolve_pending(self, eng_name, token):
        mark = "PEND" + eng_name
        for k, kind in self.pending[eng_name]:
            st = self.state.get(k)
            if not st:
                continue
            if kind == "w":
                if st["w"] == mark:
                    st["w"] = token
            else:
                st["r"] = [token if rc == mark else rc for rc in st["r"]]
        self.pending[eng_name] = []

    def op(self, eng_name, fn, reads=(), writes=(), inc=True, pe_accum=False):
        need = self._needs(reads, writes, pe_accum=pe_accum, eng_name=eng_name)
        self._do_waits(eng_name, need)
        own = "s" + eng_name
        if inc:
            self.ctr[own] += 1
            v = self.ctr[own]
            clock = dict(self.floors[eng_name])
            clock[own] = v
            self.sem_events[(own, v)] = clock
            token = {own: v}
            h = self.sems[own]
            self.ops[eng_name].append(lambda e, fn=fn, h=h: fn(e).then_inc(h, 1))
            self._resolve_pending(eng_name, token)
        else:
            token = "PEND" + eng_name
            self.ops[eng_name].append(fn)
        for k in writes:
            self.state[k] = {"w": token, "r": []}
            if not inc:
                self.pending[eng_name].append((k, "w"))
        for k in reads:
            st = self.state.setdefault(k, {"w": None, "r": []})
            st["r"].append(token)
            if not inc:
                self.pending[eng_name].append((k, "r"))
        return token

    def dma(self, eng_name, sem_name, fn, reads=(), writes=()):
        need = self._needs(reads, writes)
        self._do_waits(eng_name, need)
        h = self.dma_sems[sem_name]
        self.dma_issued[sem_name] += 16
        v = self.dma_issued[sem_name]
        clock = dict(self.floors[eng_name])
        clock[sem_name] = v
        self.sem_events[(sem_name, v)] = clock
        token = {sem_name: v}
        self.ops[eng_name].append(lambda e, fn=fn, h=h: fn(e).then_inc(h, 16))
        for k in writes:
            self.state[k] = {"w": token, "r": []}
        for k in reads:
            st = self.state.setdefault(k, {"w": None, "r": []})
            st["r"].append(token)
        return token

    def drop_keys(self, keys, rng):
        clock = {}
        for k in keys:
            st = self.state.pop(k, None)
            if st:
                if isinstance(st["w"], dict):
                    self._merge(clock, st["w"])
                for rc in st["r"]:
                    if isinstance(rc, dict):
                        self._merge(clock, rc)
        if clock:
            self.freed_clock.append((rng[0], rng[1], clock))

    def range_clock(self, rng):
        clock = {}
        for (o, e, c) in self.freed_clock:
            if o < rng[1] and rng[0] < e:
                self._merge(clock, c)
        return clock or None

    def seed(self, keys, clock):
        if clock:
            for k in keys:
                self.state[k] = {"w": dict(clock), "r": []}


def build(T):
    NT = T // 128
    NQB = T // 512
    NKC = T // 128
    assert T % 512 == 0

    nc = bass.Bass("TRN2", target_bir_lowering=False)

    d_x = nc.dram_tensor("x", [T, E], F32, kind="ExternalInput").ap()
    d_wq = nc.dram_tensor("wq", [E, E], F32, kind="ExternalInput").ap()
    d_wdn = nc.dram_tensor("wdn", [E, LAT], F32, kind="ExternalInput").ap()
    d_wku = nc.dram_tensor("wku", [LAT, E], F32, kind="ExternalInput").ap()
    d_wvu = nc.dram_tensor("wvu", [LAT, E], F32, kind="ExternalInput").ap()
    d_wo = nc.dram_tensor("wo", [E, E], F32, kind="ExternalInput").ap()
    d_wft = nc.dram_tensor("wft", [E, E], F32, kind="ExternalInput").ap()
    d_bias2 = nc.dram_tensor("bias2", [128, NE], F32, kind="ExternalInput").ap()
    d_alpha = nc.dram_tensor("alpha", [1, 1], F32, kind="ExternalInput").ap()
    d_cs = nc.dram_tensor("cs", [128, T], F32, kind="ExternalInput").ap()
    d_ss = nc.dram_tensor("ss", [128, T], F32, kind="ExternalInput").ap()
    d_pswap = nc.dram_tensor("pswap", [128, 128], BF16, kind="ExternalInput").ap()
    d_i128 = nc.dram_tensor("i128", [128, 128], BF16, kind="ExternalInput").ap()
    d_ones64 = nc.dram_tensor("ones64", [1, 64], F32, kind="ExternalInput").ap()
    d_ones128 = nc.dram_tensor("ones128", [1, 128], F32, kind="ExternalInput").ap()
    d_onescol = nc.dram_tensor("onescol", [128, 1], F32, kind="ExternalInput").ap()
    d_fc = nc.dram_tensor("fconst", [128, 4], F32, kind="ExternalInput").ap()
    d_y = nc.dram_tensor("y", [T, E], F32, kind="ExternalOutput").ap()

    with ExitStack() as stack:
        ar = Arena(nc, 207 * 1024)
        ar.attach(stack)
        pv = [stack.enter_context(nc.psum_tensor(f"pv{i}", [128, 512], F32)) for i in range(4)]
        qkA = stack.enter_context(nc.psum_tensor("qkA", [128, 1024], F32))
        qkB = stack.enter_context(nc.psum_tensor("qkB", [128, 1024], F32))

        p = Prog(nc, stack)
        for s in ("dx0", "dx1", "dx2", "dx3", "dx4", "dx5", "dx6", "dx7",
                  "dwf", "dc", "dc2", "dc3", "dg0", "dg1", "dg2", "dwo",
                  "dxr0", "dxr1", "dxr2", "dxr3", "dst0", "dst1", "dst2"):
            p.add_dma_sem(s, stack)

        SLOTS = [(pv[0], 0, "pv0"), (pv[1], 0, "pv1"), (pv[2], 0, "pv2"), (pv[3], 0, "pv3"),
                 (qkA, 0, ("qkA", 0)), (qkA, 512, ("qkA", 1)),
                 (qkB, 0, ("qkB", 0)), (qkB, 512, ("qkB", 1))]

        def challoc(name, n, shape, dtype, seeded_keys=None):
            hs = []
            for i in range(n):
                h = ar.alloc(f"{name}{i}", shape, dtype)
                hs.append(h)
                if seeded_keys is not None:
                    p.seed(seeded_keys(i), p.range_clock(ar.region(h)))
            return hs

        def chfree(hs, keysf):
            for i, h in enumerate(hs):
                p.drop_keys(keysf(i), ar.free(h))

        cI = ar.alloc("i128", [128, 128], BF16)
        cP = ar.alloc("pswap", [128, 128], BF16)
        c64b = ar.alloc("ones64b", [1, 64], BF16)
        c128 = ar.alloc("ones128", [1, 128], F32)
        cCol = ar.alloc("onescol", [128, 1], F32)
        cB2 = ar.alloc("bias2", [128, NE], F32)
        cAl = ar.alloc("alpha", [1, 1], F32)
        cAv = ar.alloc("alphav", [128, 1], F32)
        cFc = ar.alloc("fconst", [128, 4], F32)
        sSum = ar.alloc("sums", [128, NT], F32)
        sSq = ar.alloc("sumsq", [128, NT], F32)
        sMean = ar.alloc("mean", [128, NT], F32)
        sVar = ar.alloc("var", [128, NT], F32)
        sRstd = ar.alloc("rstd", [128, NT], F32)
        sNmr = ar.alloc("negmr", [128, NT], F32)
        sWabs = ar.alloc("wabs", [128, NE + 1], F32)
        sDel = ar.alloc("deltas", [1, 2], F32)
        sDbc = ar.alloc("deltabc", [128, 2], F32)

        cCs = ar.alloc("cs", [128, T], BF16)
        cSs = ar.alloc("ss", [128, T], BF16)
        NXT = 6
        xtile = challoc("x4", NXT, [128, E], F32)
        wft = challoc("wft", NE, [128, E], F32)
        xnt = challoc("xnt", 8, [128, E], BF16)
        lat = challoc("lat", NLAT, [128, T], BF16,
                      lambda i: [("lat", i, qb) for qb in range(NQB)])
        xn = challoc("xn", NE, [128, T], BF16)
        wtern = challoc("wtern", NE, [128, E], BF16)
        wdn = challoc("wdn", NE, [128, LAT], BF16)
        wku = challoc("wku", NLAT, [128, E], BF16)
        wvu = challoc("wvu", NLAT, [128, E], BF16)
        wq = challoc("wq", NE, [128, E], BF16)
        ta = ar.alloc("ta", [128, E], BF16)
        sq2 = ar.alloc("sq2", [128, E], F32)
        tb = ar.alloc("tb", [128, E], BF16)
        sq = ar.alloc("sq", [128, E], F32)

        d_x_t = d_x.rearrange("(n p) m -> n p m", p=128)
        d_wft_t = d_wft.rearrange("(k p) m -> k p m", p=128)
        def x_dma(t):
            p.dma("SP" if t % 2 == 0 else "ACT", f"dx{t % 8}",
                  lambda e, t=t: e.dma_start(
                      xtile[t % NXT][:, :], d_x_t[t, :, :]),
                  reads=[("xt4", t % NXT)], writes=[("x", t), ("xt4", t % NXT)])
        for t in range(2):
            x_dma(t)
        for ap_sb, ap_d, key in ((cFc, d_fc, "fc"), (cI, d_i128, "i128")):
            p.dma("SP", "dc", lambda e, a=ap_sb, d=ap_d: e.dma_start(a[:, :], d[:, :]),
                  writes=[(key,)])
        for t in range(2, NXT):
            x_dma(t)
        for hs, dram, key, eng, sem, n in (
                (wft, d_wft, "wft", "POOL", "dwf", NE),
                (wdn, d_wdn, "wdn", "POOL", "dg0", NE),
                (wku, d_wku, "wku", "POOL", "dg2", NLAT),
                (wvu, d_wvu, "wvu", "POOL", "dg2", NLAT),
                (wq, d_wq, "wq", "POOL", "dg1", NE)):
            d_t = dram.rearrange("(k p) m -> k p m", p=128)
            for i in range(n):
                p.dma(eng, sem, lambda e, h=hs[i], d=d_t, i=i: e.dma_start(
                    h[:, :], d[i, :, :]), writes=[(key, i)])
        for ap_sb, ap_d, key in ((cP, d_pswap, "pswap"),
                                 (c128, d_ones128, "o128"),
                                 (cCol, d_onescol, "ocol"), (cB2, d_bias2, "bias2"),
                                 (cAl, d_alpha, "alpha")):
            p.dma("POOL", "dc3", lambda e, a=ap_sb, d=ap_d: e.dma_start(a[:, :], d[:, :]),
                  writes=[(key,)])
        for ap_sb, ap_d, key in ((cCs, d_cs, "cs"), (cSs, d_ss, "ss")):
            p.dma("POOL", "dc2", lambda e, a=ap_sb, d=ap_d: e.dma_start(a[:, :], d[:, :]),
                  writes=[(key,)])
        p.op("DVE", lambda e: e.memset(c64b[0:1, 0:64], 1.0), writes=[("o64b",)])

        slot_i = [0]

        def round_emit(t4):
            for c in range(NE):
                bank, coff, bkey = SLOTS[slot_i[0] % 8]
                slot_i[0] += 1
                for j in range(4):
                    t = t4 * 4 + j
                    p.op("PE", lambda e, bank=bank, coff=coff, j=j, t=t, c=c: e.matmul(
                        bank[:, coff + j * 128: coff + (j + 1) * 128],
                        xnt[t % 8][:, c * 128:(c + 1) * 128], cI[:, :],
                        start=True, stop=True, skip_group_check=True),
                        reads=[("xnt", t), ("xnb", t % 8), ("i128",)],
                        writes=[bkey] if j in (0, 3) else [],
                        inc=(j == 3), pe_accum=True)
                p.op("DVE", lambda e, bank=bank, coff=coff, c=c, t4=t4: e.tensor_copy(
                    xn[c][:, t4 * 512:(t4 + 1) * 512], bank[:, coff:coff + 512]),
                    reads=[bkey], writes=[("xn", c, t4)])
            lbanks = []
            for j in range(NLAT):
                lbanks.append(SLOTS[slot_i[0] % 8])
                slot_i[0] += 1
            for ek in range(NE):
                for j in range(NLAT):
                    bank, coff, bkey = lbanks[j]
                    p.op("PE", lambda e, bank=bank, coff=coff, ek=ek, j=j, t4=t4: e.matmul(
                        bank[:, coff:coff + 512],
                        wdn[ek][:, j * 128:(j + 1) * 128],
                        xn[ek][:, t4 * 512:(t4 + 1) * 512],
                        start=(ek == 0), stop=(ek == NE - 1),
                        skip_group_check=True),
                        reads=[("wdn", ek), ("xn", ek, t4)],
                        writes=[bkey] if ek in (0, NE - 1) else [],
                        inc=(ek == NE - 1), pe_accum=True)
            for j in range(NLAT):
                bank, coff, bkey = lbanks[j]
                p.op("DVE", lambda e, bank=bank, coff=coff, j=j, t4=t4: e.tensor_copy(
                    lat[j][:, t4 * 512:(t4 + 1) * 512], bank[:, coff:coff + 512]),
                    reads=[bkey], writes=[("lat", j, t4)])

        for j in range(NT // 4):
            if j >= 1:
                round_emit(j - 1)
            for t in range(4 * j, 4 * j + 4):
                p.op("ACT", lambda e, t=t: e.activation(
                    sq[:, :], xtile[t % NXT][:, :], AF.Square, accum_out=sSq[:, t:t + 1]),
                    reads=[("x", t), ("xt4", t % NXT)], writes=[("sq",), ("stsq", t)])
                p.op("DVE", lambda e, t=t: e.reduce_sum(
                    sSum[:, t:t + 1], xtile[t % NXT][:, :], AX.X),
                    reads=[("x", t), ("xt4", t % NXT)], writes=[("sts", t)])
            a, b = 4 * j, 4 * j + 4
            bk = [("sts", t) for t in range(a, b)]
            bq = [("stsq", t) for t in range(a, b)]
            p.op("DVE", lambda e, a=a, b=b: e.tensor_scalar(
                sMean[:, a:b], sSum[:, a:b], 1.0 / E, None, ALU.mult),
                reads=bk, writes=[("mean", j)])
            p.op("DVE", lambda e, a=a, b=b: e.tensor_scalar(
                sSq[:, a:b], sSq[:, a:b], 1.0 / E, None, ALU.mult),
                reads=bq, writes=[("ex2", j)])
            p.op("DVE", lambda e, a=a, b=b: e.tensor_tensor(
                sVar[:, a:b], sMean[:, a:b], sMean[:, a:b], ALU.mult),
                reads=[("mean", j)], writes=[("m2", j)])
            p.op("DVE", lambda e, a=a, b=b: e.tensor_sub(
                sVar[:, a:b], sSq[:, a:b], sVar[:, a:b]),
                reads=[("ex2", j), ("m2", j)], writes=[("var", j)])
            p.op("ACT", lambda e, a=a, b=b: e.activation(
                sRstd[:, a:b], sVar[:, a:b], AF.Ln, bias=cFc[:, 0:1]),
                reads=[("var", j), ("fc",)], writes=[("lnv", j)])
            p.op("ACT", lambda e, a=a, b=b: e.activation(
                sRstd[:, a:b], sRstd[:, a:b], AF.Exp, scale=cFc[:, 1:2]),
                reads=[("lnv", j), ("fc",)], writes=[("rstd", j)])
            p.op("DVE", lambda e, a=a, b=b: e.scalar_tensor_tensor(
                sNmr[:, a:b], sMean[:, a:b], -1.0, sRstd[:, a:b], ALU.mult, ALU.mult),
                reads=[("mean", j), ("rstd", j)], writes=[("negmr", j)])
            for t in range(a, b):
                p.op("ACT", lambda e, t=t: e.activation(
                    xnt[t % 8][:, :], xtile[t % NXT][:, :], AF.Identity,
                    bias=sNmr[:, t:t + 1], scale=sRstd[:, t:t + 1]),
                    reads=[("x", t), ("xt4", t % NXT), ("rstd", j), ("negmr", j)],
                    writes=[("xnt", t), ("xnb", t % 8)])
                if t + NXT < NT:
                    x_dma(t + NXT)
        round_emit(NT // 4 - 1)
        tern_pieces = []
        tern_pieces.append(lambda: (
            p.op("PE", lambda e: e.matmul(pv[2][0:128, 0:1], c128[0:1, :], cAl[0:1, 0:1],
                                          start=True, stop=True, skip_group_check=True),
                 reads=[("alpha",), ("o128",)], writes=["pv2"]),
            p.op("DVE", lambda e: e.tensor_scalar(
                cAv[:, :], pv[2][:, 0:1], 0.5, None, ALU.mult),
                reads=["pv2"], writes=[("alphav",)])))
        for ek in range(NE):
            tern_pieces.append(lambda ek=ek: p.op("ACT", lambda e, ek=ek: e.activation(
                sq2[:, :], wft[ek][:, :], AF.Abs, accum_out=sWabs[:, ek:ek + 1]),
                reads=[("wft", ek), ("sq2",)], writes=[("sq2",), ("wabs", ek)]))

        def _delta():
            p.op("ACT", lambda e: e.activation(
                sWabs[:, 0:NE], sWabs[:, 0:NE], AF.Identity,
                accum_out=sWabs[:, NE:NE + 1]),
                reads=[("wabs", ek) for ek in range(NE)],
                writes=[("wabs", ek) for ek in range(NE)] + [("wabs1",)])
            p.op("PE", lambda e: e.matmul(pv[0][0:1, 0:1], sWabs[:, NE:NE + 1], cCol[:, :],
                                          start=True, stop=True, skip_group_check=True),
                 reads=[("wabs1",), ("ocol",)], writes=["pv0"])
            p.op("DVE", lambda e: e.tensor_scalar(
                sDel[0:1, 0:1], pv[0][0:1, 0:1], 0.7 / float(E * E), None, ALU.mult),
                reads=["pv0"], writes=[("del0",)])
            p.op("DVE", lambda e: e.tensor_scalar(
                sDel[0:1, 1:2], pv[0][0:1, 0:1], -0.7 / float(E * E), None, ALU.mult),
                reads=["pv0"], writes=[("del1",)])
            p.op("PE", lambda e: e.matmul(pv[1][0:128, 0:2], c128[0:1, :], sDel[0:1, 0:2],
                                          start=True, stop=True, skip_group_check=True),
                 reads=[("del0",), ("del1",), ("o128",)], writes=["pv1"])
            p.op("DVE", lambda e: e.tensor_copy(sDbc[:, :], pv[1][:, 0:2]),
                 reads=["pv1"], writes=[("dbc",)])
        tern_pieces.append(_delta)
        for ek in range(NE):
            def _sg(ek=ek):
                p.op("ACT", lambda e, ek=ek: e.activation(
                    ta[:, :], wft[ek][:, :], AF.Sign, bias=sDbc[:, 1:2]),
                    reads=[("wft", ek), ("dbc",)], writes=[("ta",)])
                p.op("ACT", lambda e, ek=ek: e.activation(
                    tb[:, :], wft[ek][:, :], AF.Sign, bias=sDbc[:, 0:1]),
                    reads=[("wft", ek), ("dbc",)], writes=[("tb",)])
                p.op("POOL", lambda e, ek=ek: e.tensor_add(wtern[ek][:, :], ta[:, :], tb[:, :]),
                     reads=[("ta",), ("tb",)], writes=[("wt", ek)])
                p.drop_keys([("wft", ek)], ar.free(wft[ek]))
            tern_pieces.append(_sg)

        chfree(xtile, lambda j: [("x", t) for t in range(j, NT, NXT)] + [("xt4", j)])
        chfree(xnt, lambda b: [("xnt", t) for t in range(b, NT, 8)] + [("xnb", b)])
        p.drop_keys([("sq",)], ar.free(sq))

        vsb = challoc("v", NT, [128, H, 65], BF16,
                      lambda i: [("v", i, 0), ("v", i, 1), ("vones", i)])
        stg = challoc("stg", 3, [128, 512], BF16, lambda i: [("stg", i)])
        t1b = challoc("t1", 3, [128, 512], BF16, lambda i: [("t1", i)])
        t2b = challoc("t2", 3, [128, 512], BF16, lambda i: [("t2", i)])

        def proj_fm(dst_ch, lhs_ch, lhs_keyf, n_k, out_chunks, rhs_ch, rhs_keyf, evac,
                    slot_base, lag=0, qb_major=False):
            si = [slot_base]
            groups = []
            emitted = [0]

            def flush(n):
                while emitted[0] < n:
                    evac(*groups[emitted[0]])
                    emitted[0] += 1
            if qb_major:
                ocqb = [(oc, qb) for qb in range(NQB) for oc in range(out_chunks)]
            else:
                ocqb = [(oc, qb) for oc in range(out_chunks) for qb in range(NQB)]
            for gi in range(0, len(ocqb), NQB):
                quad = ocqb[gi:gi + NQB]
                banks = [SLOTS[(si[0] + j) % 8] for j in range(len(quad))]
                si[0] += len(quad)
                for ek in range(n_k):
                    for j, (oc, qb) in enumerate(quad):
                        bank, coff, bkey = banks[j]
                        p.op("PE", lambda e, bank=bank, coff=coff, ek=ek, qb=qb, oc=oc: e.matmul(
                            bank[:, coff:coff + 512],
                            lhs_ch[ek][:, oc * 128:(oc + 1) * 128],
                            rhs_ch[ek][:, qb * 512:(qb + 1) * 512],
                            start=(ek == 0), stop=(ek == n_k - 1),
                            skip_group_check=True),
                            reads=lhs_keyf(ek) + rhs_keyf(ek, qb),
                            writes=[bkey] if ek in (0, n_k - 1) else [],
                            inc=(ek == n_k - 1), pe_accum=True)
                for j, (oc, qb) in enumerate(quad):
                    groups.append((oc, qb) + banks[j])
                flush(len(groups) - lag)
            flush(len(groups))

        def copy_evac(dst_ch, name):
            def evac(oc, qb, bank, coff, bkey):
                p.op("DVE", lambda e, bank=bank, coff=coff, oc=oc, qb=qb: e.tensor_copy(
                    dst_ch[oc][:, qb * 512:(qb + 1) * 512], bank[:, coff:coff + 512]),
                    reads=[bkey], writes=[(name, oc, qb)])
            return evac

        for tt in range(NT):
            p.op("DVE", lambda e, tt=tt: e.memset(vsb[tt][:, :, 64:65], 1.0),
                 reads=[], writes=[("vones", tt)])

        def rope_evac(dst_ch, dst_name):
            def evac(oc, qb, bank, coff, bkey):
                pp = (oc * NQB + qb) % 3
                p.op("ACT", lambda e, bank=bank, coff=coff, pp=pp: e.copy(
                    stg[pp][:, :], bank[:, coff:coff + 512]),
                    reads=[bkey], writes=[("stg", pp)])
                p.op("PE", lambda e, bank=bank, coff=coff, pp=pp: e.matmul(
                    bank[:, coff:coff + 512], cP[:, :], stg[pp][:, :],
                    start=True, stop=True, skip_group_check=True),
                    reads=[("pswap",), ("stg", pp)], writes=[bkey])
                p.op("DVE", lambda e, bank=bank, coff=coff, pp=pp, qb=qb: e.tensor_tensor(
                    t1b[pp][:, :], stg[pp][:, :], cCs[:, qb * 512:(qb + 1) * 512], ALU.mult),
                    reads=[("stg", pp), ("cs",)], writes=[("t1", pp)])
                p.op("DVE", lambda e, bank=bank, coff=coff, pp=pp, qb=qb: e.tensor_tensor(
                    t2b[pp][:, :], bank[:, coff:coff + 512],
                    cSs[:, qb * 512:(qb + 1) * 512], ALU.mult),
                    reads=[bkey, ("ss",)], writes=[("t2", pp)])
                p.op("POOL", lambda e, pp=pp, oc=oc, qb=qb: e.tensor_add(
                    dst_ch[oc][:, qb * 512:(qb + 1) * 512], t1b[pp][:, :], t2b[pp][:, :]),
                    reads=[("t1", pp), ("t2", pp)], writes=[(dst_name, oc, qb)])
            return evac

        ucnt = [0]

        krq = []

        def kr_unit(oc, qb):
            bank, coff, bkey = SLOTS[ucnt[0] % 8]
            ucnt[0] += 1
            for ek in range(NLAT):
                p.op("PE", lambda e, bank=bank, coff=coff, ek=ek, qb=qb, oc=oc: e.matmul(
                    bank[:, coff:coff + 512],
                    wku[ek][:, oc * 128:(oc + 1) * 128],
                    lat[ek][:, qb * 512:(qb + 1) * 512],
                    start=(ek == 0), stop=(ek == NLAT - 1),
                    skip_group_check=True),
                    reads=[("wku", ek), ("lat", ek, qb)],
                    writes=[bkey] if ek in (0, NLAT - 1) else [],
                    inc=(ek == NLAT - 1), pe_accum=True)
            krq.append((oc, qb, bank, coff, bkey))
            if len(krq) > 2:
                rope_evac(kr, "kr")(*krq.pop(0))

        def v_unit(tt, db):
            bank, coff, bkey = SLOTS[ucnt[0] % 8]
            ucnt[0] += 1
            for lc in range(NLAT):
                p.op("PE", lambda e, bank=bank, coff=coff, tt=tt, lc=lc, db=db: e.matmul(
                    bank[:, coff:coff + 512],
                    lat[lc][:, tt * 128:(tt + 1) * 128],
                    wvu[lc][:, db * 512:(db + 1) * 512],
                    start=(lc == 0), stop=(lc == NLAT - 1),
                    skip_group_check=True),
                    reads=[("lat", lc, (tt * 128) // 512), ("wvu", lc)],
                    writes=[bkey] if lc in (0, NLAT - 1) else [],
                    inc=(lc == NLAT - 1), pe_accum=True)
            p.op("DVE", lambda e, bank=bank, coff=coff, tt=tt, db=db: e.tensor_copy(
                vsb[tt][:, db * 8:(db + 1) * 8, 0:64], bank[:, coff:coff + 512]),
                reads=[bkey], writes=[("v", tt, db)])

        kus = [(oc, qb) for oc in range(NE) for qb in range(NQB)]
        vus = [(tt, db) for tt in range(NT) for db in range(2)]
        for i, vu in enumerate(vus):
            v_unit(*vu)
            if i < len(tern_pieces):
                tern_pieces[i]()
        p.drop_keys([("ta",)], ar.free(ta))
        p.drop_keys([("tb",)], ar.free(tb))
        p.drop_keys([("sq2",)], ar.free(sq2))
        kr = challoc("kr", NE, [128, T], BF16,
                     lambda i: [("kr", i, qb) for qb in range(NQB)])
        for ku in kus:
            kr_unit(*ku)
        while krq:
            rope_evac(kr, "kr")(*krq.pop(0))

        chfree(lat, lambda i: [("lat", i, qb) for qb in range(NQB)])
        chfree(wdn, lambda i: [("wdn", i)])
        chfree(wku, lambda i: [("wku", i)])
        chfree(wvu, lambda i: [("wvu", i)])

        g = challoc("g", NE, [128, T], BF16,
                    lambda i: [("g", i, qb) for qb in range(NQB)])

        def g_evac(oc, qb, bank, coff, bkey):
            p.op("ACT", lambda e, bank=bank, coff=coff, oc=oc, qb=qb: e.activation(
                g[oc][:, qb * 512:(qb + 1) * 512], bank[:, coff:coff + 512], AF.Silu,
                scale=cAv[:, 0:1], bias=cB2[:, oc:oc + 1]),
                reads=[bkey, ("alphav",), ("bias2",)], writes=[("g", oc, qb)])
        proj_fm(g, wtern, lambda ek: [("wt", ek)], NE, NE, xn,
                lambda ek, qb: [("xn", ek, qb)], g_evac, 0)
        chfree(wtern, lambda ek: [("wt", ek)])

        qr = challoc("qr", NE, [128, T], BF16,
                     lambda i: [("qr", i, qb) for qb in range(NQB)])
        proj_fm(qr, wq, lambda ek: [("wq", ek)], NE, NE, xn,
                lambda ek, qb: [("xn", ek, qb)], rope_evac(qr, "qr"), 0, lag=2)

        chfree(wq, lambda i: [("wq", i)])
        p.drop_keys([("cs",)], ar.free(cCs))
        p.drop_keys([("ss",)], ar.free(cSs))
        chfree(stg, lambda i: [("stg", i)])
        chfree(t1b, lambda i: [("t1", i)])
        chfree(t2b, lambda i: [("t2", i)])
        chfree(xn, lambda c: [("xn", c, t4) for t4 in range(NQB)])

        ao = challoc("ao", NE, [128, T], BF16,
                     lambda i: [("ao", 2 * i, qb) for qb in range(NQB)] +
                               [("ao", 2 * i + 1, qb) for qb in range(NQB)])
        wo = challoc("wo", NE, [128, E], BF16, lambda i: [("wo", i)])
        d_wo_t = d_wo.rearrange("(k p) m -> k p m", p=128)
        for i in range(NE):
            p.dma("POOL", "dwo", lambda e, h=wo[i], d=d_wo_t, i=i: e.dma_start(
                h[:, :], d[i, :, :]), reads=[("wo", i)], writes=[("wo", i)])
        ptb = challoc("pt", 10, [128, 1024], BF16, lambda i: [("pt", i)])
        zrb = challoc("zr", 4, [1, 512], BF16, lambda i: [("zr", i)])
        ztmp = ar.alloc("ztmp", [1, 512], F32)
        p.seed([("ztmp",)], p.range_clock(ar.region(ztmp)))

        NQH = max(1, T // 1024)

        def build_iters(h_list):
            its = []
            for hh in h_list:
                hc, hb = hh // 2, (hh % 2) * 64
                for qh in range(NQH):
                    base = qh * 1024
                    hi = min(T, base + 1024)
                    kcmax = min(NKC, (qh + 1) * 8)
                    n0 = len(its)
                    for kc in range(kcmax):
                        qstart = (kc * 128 // PAGE) * PAGE
                        lo = max(base, qstart)
                        if lo >= hi:
                            continue
                        pieces = []
                        a = lo
                        while a < hi:
                            b = min((a // 512 + 1) * 512, hi)
                            pieces.append((a, b))
                            a = b
                        its.append(dict(hh=hh, hc=hc, hb=hb, kc=kc, base=base, qh=qh,
                                        lo=lo, hi=hi, pieces=pieces, last=False))
                    if len(its) > n0:
                        its[-1]["last"] = True
            return its

        streams = [
            dict(iters=build_iters(list(range(0, H, 2))), qk=qkA,
                 qkk=[("qkA", 0), ("qkA", 1)], pvb=[pv[0], pv[1]], pvk=["pv0", "pv1"]),
            dict(iters=build_iters(list(range(1, H, 2))), qk=qkB,
                 qkk=[("qkB", 0), ("qkB", 1)], pvb=[pv[2], pv[3]], pvk=["pv2", "pv3"]),
        ]
        pti = [0]
        NPT = 10

        def emit_qk_exp(st, i):
            d = st["iters"][i]
            qk, qkk = st["qk"], st["qkk"]
            d["pt"] = pti[0] % NPT
            pti[0] += 1
            ptx = ptb[d["pt"]]
            ptkey = ("pt", d["pt"])
            hb, hc, kc, base = d["hb"], d["hc"], d["kc"], d["base"]
            lo, hi, pieces = d["lo"], d["hi"], d["pieces"]
            if hi - lo <= 512:
                st["tog"] = 1 - st.get("tog", 0)
                cb = st["tog"] * 512 - (lo - base) % 1024
                halves = [st["tog"]]
            else:
                cb = 0
                halves = sorted({(x - base) // 512 for x, _ in pieces})
            d["cb"] = cb
            wkeys = [qkk[hf] for hf in halves]
            for idx, (a, b) in enumerate(pieces):
                p.op("PE", lambda e, qk=qk, a=a, b=b, base=base, cb=cb, hb=hb, hc=hc, kc=kc: e.matmul(
                    qk[:, cb + a - base:cb + b - base],
                    kr[hc][hb:hb + 64, kc * 128:(kc + 1) * 128],
                    qr[hc][hb:hb + 64, a:b],
                    start=True, stop=True, skip_group_check=True,
                    tile_position=(hb, 0)),
                    reads=[("kr", hc, kc // 4), ("qr", hc, a // 512)],
                    writes=wkeys if idx in (0, len(pieces) - 1) else [],
                    inc=(idx == len(pieces) - 1), pe_accum=True)
            p.op("ACT", lambda e, ptx=ptx, qk=qk, lo=lo, hi=hi, base=base, cb=cb: e.activation(
                ptx[:, 0:hi - lo], qk[:, cb + lo - base:cb + hi - base], AF.Exp, scale=cFc[:, 2:3]),
                reads=wkeys + [("fc",)], writes=[ptkey])

        def emit_pv(st, i):
            d = st["iters"][i]
            ptx = ptb[d["pt"]]
            ptkey = ("pt", d["pt"])
            hh, kc, lo = d["hh"], d["kc"], d["lo"]
            for (a, b) in d["pieces"]:
                qb = a // 512
                stop = (kc == min(4 * qb + 3, NKC - 1))
                pb = st["pvb"][qb % 2]
                pk = st["pvk"][qb % 2]
                wl = []
                if kc == 0:
                    wl = [pk, ("zb", pk)]
                elif stop:
                    wl = [pk]
                p.op("PE", lambda e, pb=pb, a=a, b=b, lo=lo, ptx=ptx, kc=kc, hh=hh,
                     stop=stop: e.matmul(
                    pb[0:65, a % 512:a % 512 + (b - a)],
                    vsb[kc][:, hh, 0:65], ptx[:, a - lo:b - lo],
                    start=(kc == 0), stop=stop, skip_group_check=True),
                    reads=[("v", kc, 0), ("v", kc, 1), ("vones", kc), ptkey],
                    writes=wl, inc=stop, pe_accum=True)

        def emit_z_recip(st, i):
            d = st["iters"][i]
            hh, qh = d["hh"], d["qh"]
            for qb in range(2 * qh, min(2 * qh + 2, NQB)):
                pb = st["pvb"][qb % 2]
                pk = st["pvk"][qb % 2]
                zp = 2 * (hh % 2) + (qb % 2)
                p.op("ACT", lambda e, pb=pb: e.activation(
                    ztmp[0:1, :], pb[64:65, 0:512], AF.Ln),
                    reads=[pk, ("ztmp",)], writes=[("ztmp",)])
                p.op("ACT", lambda e, zp=zp: e.activation(
                    zrb[zp][0:1, :], ztmp[0:1, :], AF.Exp, scale=cFc[0:1, 3:4]),
                    reads=[("ztmp",), ("fc",)], writes=[("zr", zp)])

        def emit_z_apply(st, i):
            d = st["iters"][i]
            hh, hb, hc, qh = d["hh"], d["hb"], d["hc"], d["qh"]
            for qb in range(2 * qh, min(2 * qh + 2, NQB)):
                pb = st["pvb"][qb % 2]
                pk = st["pvk"][qb % 2]
                pkz = ("zb", pk)
                zp = 2 * (hh % 2) + (qb % 2)
                p.op("PE", lambda e, pb=pb, zp=zp: e.matmul(
                    pb[64:128, 0:512], c64b[0:1, 0:64], zrb[zp][0:1, :],
                    start=True, stop=True, skip_group_check=True,
                    tile_position=(0, 64)),
                    reads=[("zr", zp), ("o64b",)], writes=[pkz])
                p.op("DVE", lambda e, pb=pb, qb=qb, hb=hb, hc=hc: e.tensor_copy(
                    ao[hc][hb:hb + 64, qb * 512:(qb + 1) * 512], pb[0:64, 0:512]),
                    reads=[pk], writes=[("ao", hh, qb)])
                p.op("DVE", lambda e, pb=pb, qb=qb, hb=hb, hc=hc: e.tensor_tensor(
                    ao[hc][hb:hb + 64, qb * 512:(qb + 1) * 512],
                    ao[hc][hb:hb + 64, qb * 512:(qb + 1) * 512],
                    pb[64:128, 0:512], ALU.mult),
                    reads=[pk, pkz, ("ao", hh, qb)], writes=[("ao", hh, qb)])

        SKEW = 4
        n_s = len(streams[0]["iters"])
        assert n_s == len(streams[1]["iters"])
        pend = {id(st): [] for st in streams}

        def step(s, i):
            for st in streams:
                while pend[id(st)] and pend[id(st)][0] < i:
                    emit_z_apply(st, pend[id(st)].pop(0))
                emit_pv(st, i)
                if st["iters"][i]["last"]:
                    emit_z_recip(st, i)
                    pend[id(st)].append(i)

        for s in range(n_s):
            for st in streams:
                emit_qk_exp(st, s)
            if s >= SKEW:
                step(s, s - SKEW)
        for s in range(n_s - SKEW, n_s):
            step(s, s)
        for st in streams:
            while pend[id(st)]:
                emit_z_apply(st, pend[id(st)].pop(0))

        chfree(kr, lambda i: [("kr", i, qb) for qb in range(NQB)])
        chfree(vsb, lambda i: [("v", i, 0), ("v", i, 1), ("vones", i)])
        chfree(ptb, lambda i: [("pt", i)])
        chfree(zrb, lambda i: [("zr", i)])

        r = qr
        for oc in range(NE):
            for qb in range(NQB):
                st = p.state.pop(("qr", oc, qb), None)
                clk = {}
                if st:
                    if isinstance(st["w"], dict):
                        Prog._merge(clk, st["w"])
                    for rc in st["r"]:
                        if isinstance(rc, dict):
                            Prog._merge(clk, rc)
                if clk:
                    p.state[("r", oc, qb)] = {"w": clk, "r": []}

        xt = challoc("xt", 4, [128, E], F32, lambda i: [("xt", i)])
        ot = challoc("ot", 3, [128, E], F32, lambda i: [("ot", i, 0), ("ot", i, 1)])
        xr_sems = ["dxr0", "dxr1", "dxr2", "dxr3"]
        st_sems = ["dst0", "dst1", "dst2"]
        d_y_t = d_y.rearrange("(n p) m -> n p m", p=128)
        sc = [0]
        for qb in range(NQB):
            for oc in range(NE):
                bank, coff, bkey = SLOTS[sc[0] % 8]
                sc[0] += 1
                for ek in range(NE):
                    p.op("PE", lambda e, bank=bank, coff=coff, ek=ek, qb=qb, oc=oc: e.matmul(
                        bank[:, coff:coff + 512],
                        wo[ek][:, oc * 128:(oc + 1) * 128],
                        ao[ek][:, qb * 512:(qb + 1) * 512],
                        start=(ek == 0), stop=(ek == NE - 1),
                        skip_group_check=True),
                        reads=[("wo", ek), ("ao", 2 * ek, qb), ("ao", 2 * ek + 1, qb)],
                        writes=[bkey] if ek in (0, NE - 1) else [],
                        inc=(ek == NE - 1), pe_accum=True)
                p.op("DVE", lambda e, bank=bank, coff=coff, oc=oc, qb=qb: e.tensor_tensor(
                    r[oc][:, qb * 512:(qb + 1) * 512], bank[:, coff:coff + 512],
                    g[oc][:, qb * 512:(qb + 1) * 512], ALU.mult),
                    reads=[bkey, ("g", oc, qb)], writes=[("r", oc, qb)])
            for t in range(4 * qb, min(4 * qb + 4, NT)):
                p.dma("SP", xr_sems[t % 4], lambda e, t=t: e.dma_start(
                    xt[t % 4][:, :], d_x_t[t, :, :]),
                    reads=[("xt", t % 4)], writes=[("xt", t % 4)])
                for half in range(2):
                    bank, coff, bkey = SLOTS[sc[0] % 8]
                    sc[0] += 1
                    for j in range(4):
                        c = half * 4 + j
                        p.op("PE", lambda e, bank=bank, coff=coff, j=j, c=c, t=t: e.matmul(
                            bank[:, coff + j * 128:coff + (j + 1) * 128],
                            r[c][:, t * 128:(t + 1) * 128], cI[:, :],
                            start=True, stop=True, skip_group_check=True),
                            reads=[("r", c, (t * 128) // 512), ("i128",)],
                            writes=[bkey] if j in (0, 3) else [],
                            inc=(j == 3), pe_accum=True)
                    p.op("DVE", lambda e, bank=bank, coff=coff, t=t, half=half: e.tensor_tensor(
                        ot[t % 3][:, half * 512:(half + 1) * 512],
                        xt[t % 4][:, half * 512:(half + 1) * 512],
                        bank[:, coff:coff + 512], ALU.add),
                        reads=[bkey, ("xt", t % 4)], writes=[("ot", t % 3, half)])
                p.dma("ACT", st_sems[t % 3], lambda e, t=t: e.dma_start(
                    d_y_t[t, :, :], ot[t % 3][:, :]),
                    reads=[("ot", t % 3, 0), ("ot", t % 3, 1)])

        for s in st_sems:
            tot = p.dma_issued[s]
            if tot:
                h = p.dma_sems[s]
                p.ops["SP"].append(lambda e, h=h, tot=tot: e.wait_ge(h, tot))

        with nc.Block() as blk:
            blk.tensor(lambda e: [f(e) for f in p.ops["PE"]])
            blk.scalar(lambda e: [f(e) for f in p.ops["ACT"]])
            blk.vector(lambda e: [f(e) for f in p.ops["DVE"]])
            blk.gpsimd(lambda e: [f(e) for f in p.ops["POOL"]])
            blk.sync(lambda e: [f(e) for f in p.ops["SP"]])

    n_inst = sum(len(v) for v in p.ops.values())
    print(f"[build] T={T} insts={n_inst} (waits={p.n_waits}) arena_high={ar.high}")
    return nc


def _host_consts(T):
    inv = 1.0 / (10000.0 ** (np.arange(0, DH, 2, dtype=np.float32) / DH))
    ang = np.outer(np.arange(T, dtype=np.float32), inv)
    cos = np.cos(ang).T.astype(np.float32)
    sin = np.sin(ang).T.astype(np.float32)
    Cfull = np.ascontiguousarray(np.tile(cos, (4, 1)))
    sgn = np.where((np.arange(128) % 64) < 32, -1.0, 1.0).astype(np.float32)[:, None]
    Sfull = np.ascontiguousarray(np.tile(sin, (4, 1)) * sgn)
    Pswap = np.zeros((128, 128), dtype=ml_dtypes.bfloat16)
    idx = np.arange(128)
    Pswap[idx, idx ^ 32] = 1
    I128 = np.eye(128, dtype=ml_dtypes.bfloat16)
    perm = np.zeros(E, dtype=np.int64)
    for h in range(H):
        base = h * DH
        perm[base:base + 32] = base + np.arange(0, DH, 2)
        perm[base + 32:base + 64] = base + np.arange(1, DH, 2)
    return Cfull, Sfull, Pswap, I128, perm


def make_in_maps(L0_attn_x, L0_attn_wq, L0_attn_w_down, L0_attn_wk_up, L0_attn_wv_up,
                 L0_attn_wo, L0_ffn_weight, L0_ffn_alpha, L0_ffn_bias, T, n_cores):
    x = np.ascontiguousarray(np.asarray(L0_attn_x, dtype=np.float32))
    wq = np.asarray(L0_attn_wq, dtype=np.float32)
    wdn = np.ascontiguousarray(np.asarray(L0_attn_w_down, dtype=np.float32))
    wku = np.asarray(L0_attn_wk_up, dtype=np.float32)
    wvu = np.ascontiguousarray(np.asarray(L0_attn_wv_up, dtype=np.float32))
    wo = np.ascontiguousarray(np.asarray(L0_attn_wo, dtype=np.float32))
    wf = np.asarray(L0_ffn_weight, dtype=np.float32)
    alpha = np.ascontiguousarray(np.asarray(L0_ffn_alpha, dtype=np.float32).reshape(1, 1))
    bias = np.asarray(L0_ffn_bias, dtype=np.float32)

    Cfull, Sfull, Pswap, I128, perm = _host_consts(T)
    shared = dict(
        wq=np.ascontiguousarray(wq[:, perm]),
        wdn=wdn,
        wku=np.ascontiguousarray(wku[:, perm]),
        wvu=wvu, wo=wo,
        wft=np.ascontiguousarray(wf.T),
        bias2=np.ascontiguousarray(bias.reshape(NE, 128).T),
        alpha=alpha, cs=Cfull, ss=Sfull, pswap=Pswap, i128=I128,
        ones64=np.ones((1, 64), np.float32),
        ones128=np.ones((1, 128), np.float32),
        onescol=np.ones((128, 1), np.float32),
        fconst=np.tile(np.array([[EPS, -0.5, SCALE, -1.0]], np.float32), (128, 1)),
    )
    return [dict(shared, x=np.ascontiguousarray(x[c])) for c in range(n_cores)]


def kernel(L0_attn_x, L0_attn_wq, L0_attn_w_down, L0_attn_wk_up, L0_attn_wv_up,
           L0_attn_wo, L0_ffn_weight, L0_ffn_alpha, L0_ffn_bias):
    T = 2048
    nc = build(T)
    in_maps = make_in_maps(L0_attn_x, L0_attn_wq, L0_attn_w_down, L0_attn_wk_up,
                           L0_attn_wv_up, L0_attn_wo, L0_ffn_weight, L0_ffn_alpha,
                           L0_ffn_bias, T, N_CORES)
    res = run_bass_kernel_spmd(nc, in_maps, core_ids=list(range(N_CORES)))
    return np.stack([np.asarray(r["y"], dtype=np.float32) for r in res.results], axis=0)


# version 69
# speedup vs baseline: 1.0306x; 1.0306x over previous
import numpy as np
import ml_dtypes
from contextlib import ExitStack

import concourse.bass as bass
import concourse.mybir as mybir
from concourse.bass_utils import run_bass_kernel_spmd

F32 = mybir.dt.float32
BF16 = mybir.dt.bfloat16
U8 = mybir.dt.uint8
AF = mybir.ActivationFunctionType
ALU = mybir.AluOpType
AX = mybir.AxisListType

E = 1024
H = 16
DH = 64
LAT = 512
PAGE = 256
NE = E // 128
NLAT = LAT // 128
SCALE = 0.125
EPS = 1e-5
N_CORES = 8
KVER = 61

ENGS = ("PE", "ACT", "DVE", "POOL", "SP")


class Arena:

    def __init__(self, nc, size):
        self.nc = nc
        self.size = size
        self.free_list = [(0, size)]
        self.base = None
        self.live = {}
        self.n = 0
        self.high = 0

    def attach(self, stack):
        self.backing = stack.enter_context(
            self.nc.sbuf_tensor("arena_backing", [128, self.size], U8)
        )
        self.base = self.nc.lookup_mloc(self.backing).addr

    def alloc(self, name, shape, dtype):
        per = 1
        for s in shape[1:]:
            per *= s
        per = (per * mybir.dt.size(dtype) + 31) & ~31
        for i, (off, sz) in enumerate(self.free_list):
            if sz >= per:
                if sz == per:
                    self.free_list.pop(i)
                else:
                    self.free_list[i] = (off + per, sz - per)
                break
        else:
            raise MemoryError(f"SBUF arena OOM for {name} ({per}B); free={self.free_list}")
        self.n += 1
        uname = f"{name}_{self.n}"
        h = self.nc.virtual_sbuf_tensor(uname, shape, dtype)
        self.nc._tensor(h.name, h.shape, h.dtype, type="SB")
        mloc = self.nc.lookup_mloc(h)
        mloc.allocated = True
        mloc.addr = self.base + off
        self.live[h.name] = (off, per)
        self.high = max(self.high, off + per)
        return h

    def region(self, h):
        off, per = self.live[h.name]
        return off, off + per

    def free(self, h):
        off, per = self.live.pop(h.name)
        self.free_list.append((off, per))
        self.free_list.sort()
        merged = []
        for o, s in self.free_list:
            if merged and merged[-1][0] + merged[-1][1] == o:
                merged[-1] = (merged[-1][0], merged[-1][1] + s)
            else:
                merged.append((o, s))
        self.free_list = [tuple(x) for x in merged]
        return off, off + per


class Prog:

    def __init__(self, nc, stack):
        self.nc = nc
        self.ops = {e: [] for e in ENGS}
        self.sems = {}
        self.ctr = {}
        for e in ENGS:
            self.sems["s" + e] = stack.enter_context(nc.semaphore("s" + e))
            self.ctr["s" + e] = 0
        self.dma_sems = {}
        self.dma_issued = {}
        self.floors = {e: {} for e in ENGS}
        self.sem_events = {}
        self.state = {}
        self.pending = {e: [] for e in ENGS}
        self.n_waits = 0
        self.freed_clock = []

    def add_dma_sem(self, name, stack):
        self.dma_sems[name] = stack.enter_context(self.nc.semaphore(name))
        self.dma_issued[name] = 0

    @staticmethod
    def _merge(dst, src):
        if not src:
            return
        for s, v in src.items():
            if dst.get(s, 0) < v:
                dst[s] = v

    def _do_waits(self, eng_name, need):
        fl = self.floors[eng_name]
        for s in sorted(need, key=str):
            v = need[s]
            if s in self.dma_issued:
                v = max(v, self.dma_issued[s])
            if fl.get(s, 0) >= v:
                continue
            h = self.sems.get(s) or self.dma_sems[s]
            self.ops[eng_name].append(lambda e, h=h, v=v: e.wait_ge(h, v))
            self.n_waits += 1
            fl[s] = v
            ev = self.sem_events.get((s, v))
            if ev:
                self._merge(fl, ev)

    def _needs(self, reads, writes, pe_accum=False, eng_name=None):
        need = {}
        for k in reads:
            st = self.state.get(k)
            if st and isinstance(st["w"], dict):
                self._merge(need, st["w"])
        for k in writes:
            st = self.state.get(k)
            if st:
                w = st["w"]
                if isinstance(w, dict):
                    if not (pe_accum and eng_name == "PE" and set(w) == {"sPE"}):
                        self._merge(need, w)
                for rc in st["r"]:
                    if isinstance(rc, dict):
                        self._merge(need, rc)
        return need

    def _resolve_pending(self, eng_name, token):
        mark = "PEND" + eng_name
        for k, kind in self.pending[eng_name]:
            st = self.state.get(k)
            if not st:
                continue
            if kind == "w":
                if st["w"] == mark:
                    st["w"] = token
            else:
                st["r"] = [token if rc == mark else rc for rc in st["r"]]
        self.pending[eng_name] = []

    def op(self, eng_name, fn, reads=(), writes=(), inc=True, pe_accum=False):
        need = self._needs(reads, writes, pe_accum=pe_accum, eng_name=eng_name)
        self._do_waits(eng_name, need)
        own = "s" + eng_name
        if inc:
            self.ctr[own] += 1
            v = self.ctr[own]
            clock = dict(self.floors[eng_name])
            clock[own] = v
            self.sem_events[(own, v)] = clock
            token = {own: v}
            h = self.sems[own]
            self.ops[eng_name].append(lambda e, fn=fn, h=h: fn(e).then_inc(h, 1))
            self._resolve_pending(eng_name, token)
        else:
            token = "PEND" + eng_name
            self.ops[eng_name].append(fn)
        for k in writes:
            self.state[k] = {"w": token, "r": []}
            if not inc:
                self.pending[eng_name].append((k, "w"))
        for k in reads:
            st = self.state.setdefault(k, {"w": None, "r": []})
            st["r"].append(token)
            if not inc:
                self.pending[eng_name].append((k, "r"))
        return token

    def dma(self, eng_name, sem_name, fn, reads=(), writes=()):
        need = self._needs(reads, writes)
        self._do_waits(eng_name, need)
        h = self.dma_sems[sem_name]
        self.dma_issued[sem_name] += 16
        v = self.dma_issued[sem_name]
        clock = dict(self.floors[eng_name])
        clock[sem_name] = v
        self.sem_events[(sem_name, v)] = clock
        token = {sem_name: v}
        self.ops[eng_name].append(lambda e, fn=fn, h=h: fn(e).then_inc(h, 16))
        for k in writes:
            self.state[k] = {"w": token, "r": []}
        for k in reads:
            st = self.state.setdefault(k, {"w": None, "r": []})
            st["r"].append(token)
        return token

    def drop_keys(self, keys, rng):
        clock = {}
        for k in keys:
            st = self.state.pop(k, None)
            if st:
                if isinstance(st["w"], dict):
                    self._merge(clock, st["w"])
                for rc in st["r"]:
                    if isinstance(rc, dict):
                        self._merge(clock, rc)
        if clock:
            self.freed_clock.append((rng[0], rng[1], clock))

    def range_clock(self, rng):
        clock = {}
        for (o, e, c) in self.freed_clock:
            if o < rng[1] and rng[0] < e:
                self._merge(clock, c)
        return clock or None

    def seed(self, keys, clock):
        if clock:
            for k in keys:
                self.state[k] = {"w": dict(clock), "r": []}


def build(T):
    NT = T // 128
    NQB = T // 512
    NKC = T // 128
    assert T % 512 == 0

    nc = bass.Bass("TRN2", target_bir_lowering=False)

    d_x = nc.dram_tensor("x", [T, E], F32, kind="ExternalInput").ap()
    d_wq = nc.dram_tensor("wq", [E, E], F32, kind="ExternalInput").ap()
    d_wdn = nc.dram_tensor("wdn", [E, LAT], F32, kind="ExternalInput").ap()
    d_wku = nc.dram_tensor("wku", [LAT, E], F32, kind="ExternalInput").ap()
    d_wvu = nc.dram_tensor("wvu", [LAT, E], F32, kind="ExternalInput").ap()
    d_wo = nc.dram_tensor("wo", [E, E], F32, kind="ExternalInput").ap()
    d_wft = nc.dram_tensor("wft", [E, E], F32, kind="ExternalInput").ap()
    d_bias2 = nc.dram_tensor("bias2", [128, NE], F32, kind="ExternalInput").ap()
    d_alpha = nc.dram_tensor("alpha", [1, 1], F32, kind="ExternalInput").ap()
    d_cs = nc.dram_tensor("cs", [128, T], F32, kind="ExternalInput").ap()
    d_ss = nc.dram_tensor("ss", [128, T], F32, kind="ExternalInput").ap()
    d_pswap = nc.dram_tensor("pswap", [128, 128], BF16, kind="ExternalInput").ap()
    d_i128 = nc.dram_tensor("i128", [128, 128], BF16, kind="ExternalInput").ap()
    d_ones64 = nc.dram_tensor("ones64", [1, 64], F32, kind="ExternalInput").ap()
    d_ones128 = nc.dram_tensor("ones128", [1, 128], F32, kind="ExternalInput").ap()
    d_onescol = nc.dram_tensor("onescol", [128, 1], F32, kind="ExternalInput").ap()
    d_fc = nc.dram_tensor("fconst", [128, 4], F32, kind="ExternalInput").ap()
    d_y = nc.dram_tensor("y", [T, E], F32, kind="ExternalOutput").ap()

    with ExitStack() as stack:
        ar = Arena(nc, 207 * 1024)
        ar.attach(stack)
        pv = [stack.enter_context(nc.psum_tensor(f"pv{i}", [128, 512], F32)) for i in range(4)]
        qkA = stack.enter_context(nc.psum_tensor("qkA", [128, 1024], F32))
        qkB = stack.enter_context(nc.psum_tensor("qkB", [128, 1024], F32))

        p = Prog(nc, stack)
        for s in ("dx0", "dx1", "dx2", "dx3", "dx4", "dx5", "dx6", "dx7",
                  "dwf", "dc", "dc2", "dc3", "dg0", "dg1", "dg2", "dwo",
                  "dxr0", "dxr1", "dxr2", "dxr3", "dst0", "dst1", "dst2"):
            p.add_dma_sem(s, stack)

        SLOTS = [(pv[0], 0, "pv0"), (pv[1], 0, "pv1"), (pv[2], 0, "pv2"), (pv[3], 0, "pv3"),
                 (qkA, 0, ("qkA", 0)), (qkA, 512, ("qkA", 1)),
                 (qkB, 0, ("qkB", 0)), (qkB, 512, ("qkB", 1))]

        def challoc(name, n, shape, dtype, seeded_keys=None):
            hs = []
            for i in range(n):
                h = ar.alloc(f"{name}{i}", shape, dtype)
                hs.append(h)
                if seeded_keys is not None:
                    p.seed(seeded_keys(i), p.range_clock(ar.region(h)))
            return hs

        def chfree(hs, keysf):
            for i, h in enumerate(hs):
                p.drop_keys(keysf(i), ar.free(h))

        cI = ar.alloc("i128", [128, 128], BF16)
        cP = ar.alloc("pswap", [128, 128], BF16)
        c64b = ar.alloc("ones64b", [1, 64], BF16)
        c128 = ar.alloc("ones128", [1, 128], F32)
        cCol = ar.alloc("onescol", [128, 1], F32)
        cB2 = ar.alloc("bias2", [128, NE], F32)
        cAl = ar.alloc("alpha", [1, 1], F32)
        cAv = ar.alloc("alphav", [128, 1], F32)
        cFc = ar.alloc("fconst", [128, 4], F32)
        bst = ar.alloc("bst", [128, 12], F32)
        mvt = ar.alloc("mv", [128, NT, 2], F32)
        sRstd = ar.alloc("rstd", [128, NT], F32)
        sNmr = ar.alloc("negmr", [128, NT], F32)
        sWabs = ar.alloc("wabs", [128, NE + 1], F32)
        sDel = ar.alloc("deltas", [1, 2], F32)
        sDbc = ar.alloc("deltabc", [128, 2], F32)

        cCs = ar.alloc("cs", [128, T], BF16)
        cSs = ar.alloc("ss", [128, T], BF16)
        NXT = 6
        xtile = challoc("x4", NXT, [128, E], F32)
        wft = challoc("wft", NE, [128, E], F32)
        xnt = challoc("xnt", 8, [128, E], BF16)
        lat = challoc("lat", NLAT, [128, T], BF16,
                      lambda i: [("lat", i, qb) for qb in range(NQB)])
        xn = challoc("xn", NE, [128, T], BF16)
        wtern = challoc("wtern", NE, [128, E], BF16)
        wdn = challoc("wdn", NE, [128, LAT], BF16)
        wku = challoc("wku", NLAT, [128, E], BF16)
        wvu = challoc("wvu", NLAT, [128, E], BF16)
        wq = challoc("wq", NE, [128, E], BF16)
        ta = ar.alloc("ta", [128, E], BF16)
        sq2 = ar.alloc("sq2", [128, E], F32)
        tb = ar.alloc("tb", [128, E], BF16)

        d_x_t = d_x.rearrange("(n p) m -> n p m", p=128)
        d_wft_t = d_wft.rearrange("(k p) m -> k p m", p=128)
        def x_dma(t):
            p.dma("SP" if t % 2 == 0 else "ACT", f"dx{t % 8}",
                  lambda e, t=t: e.dma_start(
                      xtile[t % NXT][:, :], d_x_t[t, :, :]),
                  reads=[("xt4", t % NXT)], writes=[("x", t), ("xt4", t % NXT)])
        for t in range(2):
            x_dma(t)
        for ap_sb, ap_d, key in ((cFc, d_fc, "fc"), (cI, d_i128, "i128")):
            p.dma("SP", "dc", lambda e, a=ap_sb, d=ap_d: e.dma_start(a[:, :], d[:, :]),
                  writes=[(key,)])
        for t in range(2, NXT):
            x_dma(t)
        for hs, dram, key, eng, sem, n in (
                (wft, d_wft, "wft", "POOL", "dwf", NE),
                (wdn, d_wdn, "wdn", "POOL", "dg0", NE),
                (wku, d_wku, "wku", "POOL", "dg2", NLAT),
                (wvu, d_wvu, "wvu", "POOL", "dg2", NLAT),
                (wq, d_wq, "wq", "POOL", "dg1", NE)):
            d_t = dram.rearrange("(k p) m -> k p m", p=128)
            for i in range(n):
                p.dma(eng, sem, lambda e, h=hs[i], d=d_t, i=i: e.dma_start(
                    h[:, :], d[i, :, :]), writes=[(key, i)])
        for ap_sb, ap_d, key in ((cP, d_pswap, "pswap"),
                                 (c128, d_ones128, "o128"),
                                 (cCol, d_onescol, "ocol"), (cB2, d_bias2, "bias2"),
                                 (cAl, d_alpha, "alpha")):
            p.dma("POOL", "dc3", lambda e, a=ap_sb, d=ap_d: e.dma_start(a[:, :], d[:, :]),
                  writes=[(key,)])
        for ap_sb, ap_d, key in ((cCs, d_cs, "cs"), (cSs, d_ss, "ss")):
            p.dma("POOL", "dc2", lambda e, a=ap_sb, d=ap_d: e.dma_start(a[:, :], d[:, :]),
                  writes=[(key,)])
        p.op("DVE", lambda e: e.memset(c64b[0:1, 0:64], 1.0), writes=[("o64b",)])

        slot_i = [0]

        def round_emit(t4):
            for c in range(NE):
                bank, coff, bkey = SLOTS[slot_i[0] % 8]
                slot_i[0] += 1
                for j in range(4):
                    t = t4 * 4 + j
                    p.op("PE", lambda e, bank=bank, coff=coff, j=j, t=t, c=c: e.matmul(
                        bank[:, coff + j * 128: coff + (j + 1) * 128],
                        xnt[t % 8][:, c * 128:(c + 1) * 128], cI[:, :],
                        start=True, stop=True, skip_group_check=True),
                        reads=[("xnt", t), ("xnb", t % 8), ("i128",)],
                        writes=[bkey] if j in (0, 3) else [],
                        inc=(j == 3), pe_accum=True)
                p.op("DVE", lambda e, bank=bank, coff=coff, c=c, t4=t4: e.tensor_copy(
                    xn[c][:, t4 * 512:(t4 + 1) * 512], bank[:, coff:coff + 512]),
                    reads=[bkey], writes=[("xn", c, t4)])
            lbanks = []
            for j in range(NLAT):
                lbanks.append(SLOTS[slot_i[0] % 8])
                slot_i[0] += 1
            for ek in range(NE):
                for j in range(NLAT):
                    bank, coff, bkey = lbanks[j]
                    p.op("PE", lambda e, bank=bank, coff=coff, ek=ek, j=j, t4=t4: e.matmul(
                        bank[:, coff:coff + 512],
                        wdn[ek][:, j * 128:(j + 1) * 128],
                        xn[ek][:, t4 * 512:(t4 + 1) * 512],
                        start=(ek == 0), stop=(ek == NE - 1),
                        skip_group_check=True),
                        reads=[("wdn", ek), ("xn", ek, t4)],
                        writes=[bkey] if ek in (0, NE - 1) else [],
                        inc=(ek == NE - 1), pe_accum=True)
            for j in range(NLAT):
                bank, coff, bkey = lbanks[j]
                p.op("ACT", lambda e, bank=bank, coff=coff, j=j, t4=t4: e.copy(
                    lat[j][:, t4 * 512:(t4 + 1) * 512], bank[:, coff:coff + 512]),
                    reads=[bkey], writes=[("lat", j, t4)])

        for j in range(NT // 4):
            for t in range(4 * j, 4 * j + 4):
                p.op("DVE", lambda e, t=t: e.bn_stats(
                    bst[:, 0:6], xtile[t % NXT][:, 0:512]),
                    reads=[("x", t), ("xt4", t % NXT)], writes=[("bst", 0)])
                p.op("DVE", lambda e, t=t: e.bn_stats(
                    bst[:, 6:12], xtile[t % NXT][:, 512:1024]),
                    reads=[("x", t), ("xt4", t % NXT)], writes=[("bst", 1)])
                p.op("DVE", lambda e, t=t: e.bn_aggr(
                    mvt[:, t, :], bst[:, :]),
                    reads=[("bst", 0), ("bst", 1)], writes=[("mv", t)])
            a, b = 4 * j, 4 * j + 4
            bk = [("mv", t) for t in range(a, b)]
            p.op("ACT", lambda e, a=a, b=b: e.activation(
                sRstd[:, a:b], mvt[:, a:b, 1], AF.Ln, bias=cFc[:, 0:1]),
                reads=bk + [("fc",)], writes=[("lnv", j)])
            p.op("ACT", lambda e, a=a, b=b: e.activation(
                sRstd[:, a:b], sRstd[:, a:b], AF.Exp, scale=cFc[:, 1:2]),
                reads=[("lnv", j), ("fc",)], writes=[("rstd", j)])
            p.op("DVE", lambda e, a=a, b=b: e.scalar_tensor_tensor(
                sNmr[:, a:b], mvt[:, a:b, 0], -1.0, sRstd[:, a:b], ALU.mult, ALU.mult),
                reads=bk + [("rstd", j)], writes=[("negmr", j)])
            for t in range(a, b):
                p.op("ACT", lambda e, t=t: e.activation(
                    xnt[t % 8][:, :], xtile[t % NXT][:, :], AF.Identity,
                    bias=sNmr[:, t:t + 1], scale=sRstd[:, t:t + 1]),
                    reads=[("x", t), ("xt4", t % NXT), ("rstd", j), ("negmr", j)],
                    writes=[("xnt", t), ("xnb", t % 8)])
                if t + NXT < NT:
                    x_dma(t + NXT)
            if j >= 1:
                round_emit(j - 1)
        round_emit(NT // 4 - 1)
        tern_pieces = []
        tern_pieces.append(lambda: (
            p.op("PE", lambda e: e.matmul(pv[2][0:128, 0:1], c128[0:1, :], cAl[0:1, 0:1],
                                          start=True, stop=True, skip_group_check=True),
                 reads=[("alpha",), ("o128",)], writes=["pv2"]),
            p.op("DVE", lambda e: e.tensor_scalar(
                cAv[:, :], pv[2][:, 0:1], 0.5, None, ALU.mult),
                reads=["pv2"], writes=[("alphav",)])))
        for ek in range(NE):
            tern_pieces.append(lambda ek=ek: p.op("ACT", lambda e, ek=ek: e.activation(
                sq2[:, :], wft[ek][:, :], AF.Abs, accum_out=sWabs[:, ek:ek + 1]),
                reads=[("wft", ek), ("sq2",)], writes=[("sq2",), ("wabs", ek)]))

        def _delta():
            p.op("ACT", lambda e: e.activation(
                sWabs[:, 0:NE], sWabs[:, 0:NE], AF.Identity,
                accum_out=sWabs[:, NE:NE + 1]),
                reads=[("wabs", ek) for ek in range(NE)],
                writes=[("wabs", ek) for ek in range(NE)] + [("wabs1",)])
            p.op("PE", lambda e: e.matmul(pv[0][0:1, 0:1], sWabs[:, NE:NE + 1], cCol[:, :],
                                          start=True, stop=True, skip_group_check=True),
                 reads=[("wabs1",), ("ocol",)], writes=["pv0"])
            p.op("DVE", lambda e: e.tensor_scalar(
                sDel[0:1, 0:1], pv[0][0:1, 0:1], 0.7 / float(E * E), None, ALU.mult),
                reads=["pv0"], writes=[("del0",)])
            p.op("DVE", lambda e: e.tensor_scalar(
                sDel[0:1, 1:2], pv[0][0:1, 0:1], -0.7 / float(E * E), None, ALU.mult),
                reads=["pv0"], writes=[("del1",)])
            p.op("PE", lambda e: e.matmul(pv[1][0:128, 0:2], c128[0:1, :], sDel[0:1, 0:2],
                                          start=True, stop=True, skip_group_check=True),
                 reads=[("del0",), ("del1",), ("o128",)], writes=["pv1"])
            p.op("DVE", lambda e: e.tensor_copy(sDbc[:, :], pv[1][:, 0:2]),
                 reads=["pv1"], writes=[("dbc",)])
        tern_pieces.append(_delta)
        for ek in range(NE):
            def _sg(ek=ek):
                p.op("ACT", lambda e, ek=ek: e.activation(
                    ta[:, :], wft[ek][:, :], AF.Sign, bias=sDbc[:, 1:2]),
                    reads=[("wft", ek), ("dbc",)], writes=[("ta",)])
                p.op("ACT", lambda e, ek=ek: e.activation(
                    tb[:, :], wft[ek][:, :], AF.Sign, bias=sDbc[:, 0:1]),
                    reads=[("wft", ek), ("dbc",)], writes=[("tb",)])
                p.op("POOL", lambda e, ek=ek: e.tensor_add(wtern[ek][:, :], ta[:, :], tb[:, :]),
                     reads=[("ta",), ("tb",)], writes=[("wt", ek)])
                p.drop_keys([("wft", ek)], ar.free(wft[ek]))
            tern_pieces.append(_sg)

        chfree(xtile, lambda j: [("x", t) for t in range(j, NT, NXT)] + [("xt4", j)])
        chfree(xnt, lambda b: [("xnt", t) for t in range(b, NT, 8)] + [("xnb", b)])

        vsb = challoc("v", NT, [128, H, 65], BF16,
                      lambda i: [("v", i, 0), ("v", i, 1), ("vones", i)])
        stg = challoc("stg", 3, [128, 512], BF16, lambda i: [("stg", i)])
        t1b = challoc("t1", 3, [128, 512], BF16, lambda i: [("t1", i)])
        t2b = challoc("t2", 3, [128, 512], BF16, lambda i: [("t2", i)])

        def proj_fm(dst_ch, lhs_ch, lhs_keyf, n_k, out_chunks, rhs_ch, rhs_keyf, evac,
                    slot_base, lag=0, qb_major=False):
            si = [slot_base]
            groups = []
            emitted = [0]

            def flush(n):
                while emitted[0] < n:
                    evac(*groups[emitted[0]])
                    emitted[0] += 1
            if qb_major:
                ocqb = [(oc, qb) for qb in range(NQB) for oc in range(out_chunks)]
            else:
                ocqb = [(oc, qb) for oc in range(out_chunks) for qb in range(NQB)]
            for gi in range(0, len(ocqb), NQB):
                quad = ocqb[gi:gi + NQB]
                banks = [SLOTS[(si[0] + j) % 8] for j in range(len(quad))]
                si[0] += len(quad)
                for ek in range(n_k):
                    for j, (oc, qb) in enumerate(quad):
                        bank, coff, bkey = banks[j]
                        p.op("PE", lambda e, bank=bank, coff=coff, ek=ek, qb=qb, oc=oc: e.matmul(
                            bank[:, coff:coff + 512],
                            lhs_ch[ek][:, oc * 128:(oc + 1) * 128],
                            rhs_ch[ek][:, qb * 512:(qb + 1) * 512],
                            start=(ek == 0), stop=(ek == n_k - 1),
                            skip_group_check=True),
                            reads=lhs_keyf(ek) + rhs_keyf(ek, qb),
                            writes=[bkey] if ek in (0, n_k - 1) else [],
                            inc=(ek == n_k - 1), pe_accum=True)
                for j, (oc, qb) in enumerate(quad):
                    groups.append((oc, qb) + banks[j])
                flush(len(groups) - lag)
            flush(len(groups))

        def copy_evac(dst_ch, name):
            def evac(oc, qb, bank, coff, bkey):
                p.op("DVE", lambda e, bank=bank, coff=coff, oc=oc, qb=qb: e.tensor_copy(
                    dst_ch[oc][:, qb * 512:(qb + 1) * 512], bank[:, coff:coff + 512]),
                    reads=[bkey], writes=[(name, oc, qb)])
            return evac

        for tt in range(NT):
            p.op("DVE", lambda e, tt=tt: e.memset(vsb[tt][:, :, 64:65], 1.0),
                 reads=[], writes=[("vones", tt)])

        def rope_evac(dst_ch, dst_name):
            def evac(oc, qb, bank, coff, bkey):
                pp = (oc * NQB + qb) % 3
                p.op("ACT", lambda e, bank=bank, coff=coff, pp=pp: e.copy(
                    stg[pp][:, :], bank[:, coff:coff + 512]),
                    reads=[bkey], writes=[("stg", pp)])
                p.op("PE", lambda e, bank=bank, coff=coff, pp=pp: e.matmul(
                    bank[:, coff:coff + 512], cP[:, :], stg[pp][:, :],
                    start=True, stop=True, skip_group_check=True),
                    reads=[("pswap",), ("stg", pp)], writes=[bkey])
                p.op("DVE", lambda e, bank=bank, coff=coff, pp=pp, qb=qb: e.tensor_tensor(
                    t1b[pp][:, :], stg[pp][:, :], cCs[:, qb * 512:(qb + 1) * 512], ALU.mult),
                    reads=[("stg", pp), ("cs",)], writes=[("t1", pp)])
                p.op("DVE", lambda e, bank=bank, coff=coff, pp=pp, qb=qb: e.tensor_tensor(
                    t2b[pp][:, :], bank[:, coff:coff + 512],
                    cSs[:, qb * 512:(qb + 1) * 512], ALU.mult),
                    reads=[bkey, ("ss",)], writes=[("t2", pp)])
                p.op("POOL", lambda e, pp=pp, oc=oc, qb=qb: e.tensor_add(
                    dst_ch[oc][:, qb * 512:(qb + 1) * 512], t1b[pp][:, :], t2b[pp][:, :]),
                    reads=[("t1", pp), ("t2", pp)], writes=[(dst_name, oc, qb)])
            return evac

        ucnt = [0]

        krq = []

        def kr_unit(oc, qb):
            bank, coff, bkey = SLOTS[ucnt[0] % 8]
            ucnt[0] += 1
            for ek in range(NLAT):
                p.op("PE", lambda e, bank=bank, coff=coff, ek=ek, qb=qb, oc=oc: e.matmul(
                    bank[:, coff:coff + 512],
                    wku[ek][:, oc * 128:(oc + 1) * 128],
                    lat[ek][:, qb * 512:(qb + 1) * 512],
                    start=(ek == 0), stop=(ek == NLAT - 1),
                    skip_group_check=True),
                    reads=[("wku", ek), ("lat", ek, qb)],
                    writes=[bkey] if ek in (0, NLAT - 1) else [],
                    inc=(ek == NLAT - 1), pe_accum=True)
            krq.append((oc, qb, bank, coff, bkey))
            if len(krq) > 2:
                rope_evac(kr, "kr")(*krq.pop(0))

        def v_unit(tt, db):
            bank, coff, bkey = SLOTS[ucnt[0] % 8]
            ucnt[0] += 1
            for lc in range(NLAT):
                p.op("PE", lambda e, bank=bank, coff=coff, tt=tt, lc=lc, db=db: e.matmul(
                    bank[:, coff:coff + 512],
                    lat[lc][:, tt * 128:(tt + 1) * 128],
                    wvu[lc][:, db * 512:(db + 1) * 512],
                    start=(lc == 0), stop=(lc == NLAT - 1),
                    skip_group_check=True),
                    reads=[("lat", lc, (tt * 128) // 512), ("wvu", lc)],
                    writes=[bkey] if lc in (0, NLAT - 1) else [],
                    inc=(lc == NLAT - 1), pe_accum=True)
            p.op("DVE", lambda e, bank=bank, coff=coff, tt=tt, db=db: e.tensor_copy(
                vsb[tt][:, db * 8:(db + 1) * 8, 0:64], bank[:, coff:coff + 512]),
                reads=[bkey], writes=[("v", tt, db)])

        kus = [(oc, qb) for oc in range(NE) for qb in range(NQB)]
        vus = [(tt, db) for tt in range(NT) for db in range(2)]
        for i, vu in enumerate(vus):
            v_unit(*vu)
            if i < len(tern_pieces):
                tern_pieces[i]()
        p.drop_keys([("ta",)], ar.free(ta))
        p.drop_keys([("tb",)], ar.free(tb))
        p.drop_keys([("sq2",)], ar.free(sq2))
        kr = challoc("kr", NE, [128, T], BF16,
                     lambda i: [("kr", i, qb) for qb in range(NQB)])
        for ku in kus:
            kr_unit(*ku)
        while krq:
            rope_evac(kr, "kr")(*krq.pop(0))

        chfree(lat, lambda i: [("lat", i, qb) for qb in range(NQB)])
        chfree(wdn, lambda i: [("wdn", i)])
        chfree(wku, lambda i: [("wku", i)])
        chfree(wvu, lambda i: [("wvu", i)])

        g = challoc("g", NE, [128, T], BF16,
                    lambda i: [("g", i, qb) for qb in range(NQB)])

        def g_evac(oc, qb, bank, coff, bkey):
            p.op("ACT", lambda e, bank=bank, coff=coff, oc=oc, qb=qb: e.activation(
                g[oc][:, qb * 512:(qb + 1) * 512], bank[:, coff:coff + 512], AF.Silu,
                scale=cAv[:, 0:1], bias=cB2[:, oc:oc + 1]),
                reads=[bkey, ("alphav",), ("bias2",)], writes=[("g", oc, qb)])
        proj_fm(g, wtern, lambda ek: [("wt", ek)], NE, NE, xn,
                lambda ek, qb: [("xn", ek, qb)], g_evac, 0)
        chfree(wtern, lambda ek: [("wt", ek)])

        qr = challoc("qr", NE, [128, T], BF16,
                     lambda i: [("qr", i, qb) for qb in range(NQB)])
        proj_fm(qr, wq, lambda ek: [("wq", ek)], NE, NE, xn,
                lambda ek, qb: [("xn", ek, qb)], rope_evac(qr, "qr"), 0, lag=2)

        chfree(wq, lambda i: [("wq", i)])
        p.drop_keys([("cs",)], ar.free(cCs))
        p.drop_keys([("ss",)], ar.free(cSs))
        chfree(stg, lambda i: [("stg", i)])
        chfree(t1b, lambda i: [("t1", i)])
        chfree(t2b, lambda i: [("t2", i)])
        chfree(xn, lambda c: [("xn", c, t4) for t4 in range(NQB)])

        ao = challoc("ao", NE, [128, T], BF16,
                     lambda i: [("ao", 2 * i, qb) for qb in range(NQB)] +
                               [("ao", 2 * i + 1, qb) for qb in range(NQB)])
        wo = challoc("wo", NE, [128, E], BF16, lambda i: [("wo", i)])
        d_wo_t = d_wo.rearrange("(k p) m -> k p m", p=128)
        for i in range(NE):
            p.dma("POOL", "dwo", lambda e, h=wo[i], d=d_wo_t, i=i: e.dma_start(
                h[:, :], d[i, :, :]), reads=[("wo", i)], writes=[("wo", i)])
        ptb = challoc("pt", 10, [128, 1024], BF16, lambda i: [("pt", i)])
        zrb = challoc("zr", 4, [1, 512], BF16, lambda i: [("zr", i)])
        ztmp = ar.alloc("ztmp", [1, 512], F32)
        p.seed([("ztmp",)], p.range_clock(ar.region(ztmp)))

        NQH = max(1, T // 1024)

        def build_iters(h_list):
            its = []
            for hh in h_list:
                hc, hb = hh // 2, (hh % 2) * 64
                for qh in range(NQH):
                    base = qh * 1024
                    hi = min(T, base + 1024)
                    kcmax = min(NKC, (qh + 1) * 8)
                    n0 = len(its)
                    for kc in range(kcmax):
                        qstart = (kc * 128 // PAGE) * PAGE
                        lo = max(base, qstart)
                        if lo >= hi:
                            continue
                        pieces = []
                        a = lo
                        while a < hi:
                            b = min((a // 512 + 1) * 512, hi)
                            pieces.append((a, b))
                            a = b
                        its.append(dict(hh=hh, hc=hc, hb=hb, kc=kc, base=base, qh=qh,
                                        lo=lo, hi=hi, pieces=pieces, last=False))
                    if len(its) > n0:
                        its[-1]["last"] = True
            return its

        streams = [
            dict(iters=build_iters(list(range(0, H, 2))), qk=qkA,
                 qkk=[("qkA", 0), ("qkA", 1)], pvb=[pv[0], pv[1]], pvk=["pv0", "pv1"]),
            dict(iters=build_iters(list(range(1, H, 2))), qk=qkB,
                 qkk=[("qkB", 0), ("qkB", 1)], pvb=[pv[2], pv[3]], pvk=["pv2", "pv3"]),
        ]
        pti = [0]
        NPT = 10

        def emit_qk_exp(st, i):
            d = st["iters"][i]
            qk, qkk = st["qk"], st["qkk"]
            d["pt"] = pti[0] % NPT
            pti[0] += 1
            ptx = ptb[d["pt"]]
            ptkey = ("pt", d["pt"])
            hb, hc, kc, base = d["hb"], d["hc"], d["kc"], d["base"]
            lo, hi, pieces = d["lo"], d["hi"], d["pieces"]
            if hi - lo <= 512:
                st["tog"] = 1 - st.get("tog", 0)
                cb = st["tog"] * 512 - (lo - base) % 1024
                halves = [st["tog"]]
            else:
                cb = 0
                halves = sorted({(x - base) // 512 for x, _ in pieces})
            d["cb"] = cb
            wkeys = [qkk[hf] for hf in halves]
            for idx, (a, b) in enumerate(pieces):
                p.op("PE", lambda e, qk=qk, a=a, b=b, base=base, cb=cb, hb=hb, hc=hc, kc=kc: e.matmul(
                    qk[:, cb + a - base:cb + b - base],
                    kr[hc][hb:hb + 64, kc * 128:(kc + 1) * 128],
                    qr[hc][hb:hb + 64, a:b],
                    start=True, stop=True, skip_group_check=True,
                    tile_position=(hb, 0)),
                    reads=[("kr", hc, kc // 4), ("qr", hc, a // 512)],
                    writes=wkeys if idx in (0, len(pieces) - 1) else [],
                    inc=(idx == len(pieces) - 1), pe_accum=True)
            p.op("ACT", lambda e, ptx=ptx, qk=qk, lo=lo, hi=hi, base=base, cb=cb: e.activation(
                ptx[:, 0:hi - lo], qk[:, cb + lo - base:cb + hi - base], AF.Exp, scale=cFc[:, 2:3]),
                reads=wkeys + [("fc",)], writes=[ptkey])

        def emit_pv(st, i):
            d = st["iters"][i]
            ptx = ptb[d["pt"]]
            ptkey = ("pt", d["pt"])
            hh, kc, lo = d["hh"], d["kc"], d["lo"]
            for (a, b) in d["pieces"]:
                qb = a // 512
                stop = (kc == min(4 * qb + 3, NKC - 1))
                pb = st["pvb"][qb % 2]
                pk = st["pvk"][qb % 2]
                wl = []
                if kc == 0:
                    wl = [pk, ("zb", pk)]
                elif stop:
                    wl = [pk]
                p.op("PE", lambda e, pb=pb, a=a, b=b, lo=lo, ptx=ptx, kc=kc, hh=hh,
                     stop=stop: e.matmul(
                    pb[0:65, a % 512:a % 512 + (b - a)],
                    vsb[kc][:, hh, 0:65], ptx[:, a - lo:b - lo],
                    start=(kc == 0), stop=stop, skip_group_check=True),
                    reads=[("v", kc, 0), ("v", kc, 1), ("vones", kc), ptkey],
                    writes=wl, inc=stop, pe_accum=True)

        def emit_z_recip(st, i):
            d = st["iters"][i]
            hh, qh = d["hh"], d["qh"]
            for qb in range(2 * qh, min(2 * qh + 2, NQB)):
                pb = st["pvb"][qb % 2]
                pk = st["pvk"][qb % 2]
                zp = 2 * (hh % 2) + (qb % 2)
                p.op("ACT", lambda e, pb=pb: e.activation(
                    ztmp[0:1, :], pb[64:65, 0:512], AF.Ln),
                    reads=[pk, ("ztmp",)], writes=[("ztmp",)])
                p.op("ACT", lambda e, zp=zp: e.activation(
                    zrb[zp][0:1, :], ztmp[0:1, :], AF.Exp, scale=cFc[0:1, 3:4]),
                    reads=[("ztmp",), ("fc",)], writes=[("zr", zp)])

        def emit_z_apply(st, i):
            d = st["iters"][i]
            hh, hb, hc, qh = d["hh"], d["hb"], d["hc"], d["qh"]
            for qb in range(2 * qh, min(2 * qh + 2, NQB)):
                pb = st["pvb"][qb % 2]
                pk = st["pvk"][qb % 2]
                pkz = ("zb", pk)
                zp = 2 * (hh % 2) + (qb % 2)
                p.op("PE", lambda e, pb=pb, zp=zp: e.matmul(
                    pb[64:128, 0:512], c64b[0:1, 0:64], zrb[zp][0:1, :],
                    start=True, stop=True, skip_group_check=True,
                    tile_position=(0, 64)),
                    reads=[("zr", zp), ("o64b",)], writes=[pkz])
                p.op("DVE", lambda e, pb=pb, qb=qb, hb=hb, hc=hc: e.tensor_copy(
                    ao[hc][hb:hb + 64, qb * 512:(qb + 1) * 512], pb[0:64, 0:512]),
                    reads=[pk], writes=[("ao", hh, qb)])
                p.op("DVE", lambda e, pb=pb, qb=qb, hb=hb, hc=hc: e.tensor_tensor(
                    ao[hc][hb:hb + 64, qb * 512:(qb + 1) * 512],
                    ao[hc][hb:hb + 64, qb * 512:(qb + 1) * 512],
                    pb[64:128, 0:512], ALU.mult),
                    reads=[pk, pkz, ("ao", hh, qb)], writes=[("ao", hh, qb)])

        SKEW = 4
        n_s = len(streams[0]["iters"])
        assert n_s == len(streams[1]["iters"])
        pend = {id(st): [] for st in streams}

        def step(s, i):
            for st in streams:
                while pend[id(st)] and pend[id(st)][0] < i:
                    emit_z_apply(st, pend[id(st)].pop(0))
                emit_pv(st, i)
                if st["iters"][i]["last"]:
                    emit_z_recip(st, i)
                    pend[id(st)].append(i)

        for s in range(n_s):
            for st in streams:
                emit_qk_exp(st, s)
            if s >= SKEW:
                step(s, s - SKEW)
        for s in range(n_s - SKEW, n_s):
            step(s, s)
        for st in streams:
            while pend[id(st)]:
                emit_z_apply(st, pend[id(st)].pop(0))

        chfree(kr, lambda i: [("kr", i, qb) for qb in range(NQB)])
        chfree(vsb, lambda i: [("v", i, 0), ("v", i, 1), ("vones", i)])
        chfree(ptb, lambda i: [("pt", i)])
        chfree(zrb, lambda i: [("zr", i)])

        r = qr
        for oc in range(NE):
            for qb in range(NQB):
                st = p.state.pop(("qr", oc, qb), None)
                clk = {}
                if st:
                    if isinstance(st["w"], dict):
                        Prog._merge(clk, st["w"])
                    for rc in st["r"]:
                        if isinstance(rc, dict):
                            Prog._merge(clk, rc)
                if clk:
                    p.state[("r", oc, qb)] = {"w": clk, "r": []}

        xt = challoc("xt", 4, [128, E], F32, lambda i: [("xt", i)])
        ot = challoc("ot", 3, [128, E], F32, lambda i: [("ot", i, 0), ("ot", i, 1)])
        xr_sems = ["dxr0", "dxr1", "dxr2", "dxr3"]
        st_sems = ["dst0", "dst1", "dst2"]
        d_y_t = d_y.rearrange("(n p) m -> n p m", p=128)
        sc = [0]
        for qb in range(NQB):
            for oc in range(NE):
                bank, coff, bkey = SLOTS[sc[0] % 8]
                sc[0] += 1
                for ek in range(NE):
                    p.op("PE", lambda e, bank=bank, coff=coff, ek=ek, qb=qb, oc=oc: e.matmul(
                        bank[:, coff:coff + 512],
                        wo[ek][:, oc * 128:(oc + 1) * 128],
                        ao[ek][:, qb * 512:(qb + 1) * 512],
                        start=(ek == 0), stop=(ek == NE - 1),
                        skip_group_check=True),
                        reads=[("wo", ek), ("ao", 2 * ek, qb), ("ao", 2 * ek + 1, qb)],
                        writes=[bkey] if ek in (0, NE - 1) else [],
                        inc=(ek == NE - 1), pe_accum=True)
                p.op("DVE", lambda e, bank=bank, coff=coff, oc=oc, qb=qb: e.tensor_tensor(
                    r[oc][:, qb * 512:(qb + 1) * 512], bank[:, coff:coff + 512],
                    g[oc][:, qb * 512:(qb + 1) * 512], ALU.mult),
                    reads=[bkey, ("g", oc, qb)], writes=[("r", oc, qb)])
            for t in range(4 * qb, min(4 * qb + 4, NT)):
                p.dma("SP", xr_sems[t % 4], lambda e, t=t: e.dma_start(
                    xt[t % 4][:, :], d_x_t[t, :, :]),
                    reads=[("xt", t % 4)], writes=[("xt", t % 4)])
                for half in range(2):
                    bank, coff, bkey = SLOTS[sc[0] % 8]
                    sc[0] += 1
                    for j in range(4):
                        c = half * 4 + j
                        p.op("PE", lambda e, bank=bank, coff=coff, j=j, c=c, t=t: e.matmul(
                            bank[:, coff + j * 128:coff + (j + 1) * 128],
                            r[c][:, t * 128:(t + 1) * 128], cI[:, :],
                            start=True, stop=True, skip_group_check=True),
                            reads=[("r", c, (t * 128) // 512), ("i128",)],
                            writes=[bkey] if j in (0, 3) else [],
                            inc=(j == 3), pe_accum=True)
                    p.op("DVE", lambda e, bank=bank, coff=coff, t=t, half=half: e.tensor_tensor(
                        ot[t % 3][:, half * 512:(half + 1) * 512],
                        xt[t % 4][:, half * 512:(half + 1) * 512],
                        bank[:, coff:coff + 512], ALU.add),
                        reads=[bkey, ("xt", t % 4)], writes=[("ot", t % 3, half)])
                p.dma("ACT", st_sems[t % 3], lambda e, t=t: e.dma_start(
                    d_y_t[t, :, :], ot[t % 3][:, :]),
                    reads=[("ot", t % 3, 0), ("ot", t % 3, 1)])

        for s in st_sems:
            tot = p.dma_issued[s]
            if tot:
                h = p.dma_sems[s]
                p.ops["SP"].append(lambda e, h=h, tot=tot: e.wait_ge(h, tot))

        with nc.Block() as blk:
            blk.tensor(lambda e: [f(e) for f in p.ops["PE"]])
            blk.scalar(lambda e: [f(e) for f in p.ops["ACT"]])
            blk.vector(lambda e: [f(e) for f in p.ops["DVE"]])
            blk.gpsimd(lambda e: [f(e) for f in p.ops["POOL"]])
            blk.sync(lambda e: [f(e) for f in p.ops["SP"]])

    n_inst = sum(len(v) for v in p.ops.values())
    print(f"[build] T={T} insts={n_inst} (waits={p.n_waits}) arena_high={ar.high}")
    return nc


def _host_consts(T):
    inv = 1.0 / (10000.0 ** (np.arange(0, DH, 2, dtype=np.float32) / DH))
    ang = np.outer(np.arange(T, dtype=np.float32), inv)
    cos = np.cos(ang).T.astype(np.float32)
    sin = np.sin(ang).T.astype(np.float32)
    Cfull = np.ascontiguousarray(np.tile(cos, (4, 1)))
    sgn = np.where((np.arange(128) % 64) < 32, -1.0, 1.0).astype(np.float32)[:, None]
    Sfull = np.ascontiguousarray(np.tile(sin, (4, 1)) * sgn)
    Pswap = np.zeros((128, 128), dtype=ml_dtypes.bfloat16)
    idx = np.arange(128)
    Pswap[idx, idx ^ 32] = 1
    I128 = np.eye(128, dtype=ml_dtypes.bfloat16)
    perm = np.zeros(E, dtype=np.int64)
    for h in range(H):
        base = h * DH
        perm[base:base + 32] = base + np.arange(0, DH, 2)
        perm[base + 32:base + 64] = base + np.arange(1, DH, 2)
    return Cfull, Sfull, Pswap, I128, perm


def make_in_maps(L0_attn_x, L0_attn_wq, L0_attn_w_down, L0_attn_wk_up, L0_attn_wv_up,
                 L0_attn_wo, L0_ffn_weight, L0_ffn_alpha, L0_ffn_bias, T, n_cores):
    x = np.ascontiguousarray(np.asarray(L0_attn_x, dtype=np.float32))
    wq = np.asarray(L0_attn_wq, dtype=np.float32)
    wdn = np.ascontiguousarray(np.asarray(L0_attn_w_down, dtype=np.float32))
    wku = np.asarray(L0_attn_wk_up, dtype=np.float32)
    wvu = np.ascontiguousarray(np.asarray(L0_attn_wv_up, dtype=np.float32))
    wo = np.ascontiguousarray(np.asarray(L0_attn_wo, dtype=np.float32))
    wf = np.asarray(L0_ffn_weight, dtype=np.float32)
    alpha = np.ascontiguousarray(np.asarray(L0_ffn_alpha, dtype=np.float32).reshape(1, 1))
    bias = np.asarray(L0_ffn_bias, dtype=np.float32)

    Cfull, Sfull, Pswap, I128, perm = _host_consts(T)
    shared = dict(
        wq=np.ascontiguousarray(wq[:, perm]),
        wdn=wdn,
        wku=np.ascontiguousarray(wku[:, perm]),
        wvu=wvu, wo=wo,
        wft=np.ascontiguousarray(wf.T),
        bias2=np.ascontiguousarray(bias.reshape(NE, 128).T),
        alpha=alpha, cs=Cfull, ss=Sfull, pswap=Pswap, i128=I128,
        ones64=np.ones((1, 64), np.float32),
        ones128=np.ones((1, 128), np.float32),
        onescol=np.ones((128, 1), np.float32),
        fconst=np.tile(np.array([[EPS, -0.5, SCALE, -1.0]], np.float32), (128, 1)),
    )
    return [dict(shared, x=np.ascontiguousarray(x[c])) for c in range(n_cores)]


def kernel(L0_attn_x, L0_attn_wq, L0_attn_w_down, L0_attn_wk_up, L0_attn_wv_up,
           L0_attn_wo, L0_ffn_weight, L0_ffn_alpha, L0_ffn_bias):
    T = 2048
    nc = build(T)
    in_maps = make_in_maps(L0_attn_x, L0_attn_wq, L0_attn_w_down, L0_attn_wk_up,
                           L0_attn_wv_up, L0_attn_wo, L0_ffn_weight, L0_ffn_alpha,
                           L0_ffn_bias, T, N_CORES)
    res = run_bass_kernel_spmd(nc, in_maps, core_ids=list(range(N_CORES)))
    return np.stack([np.asarray(r["y"], dtype=np.float32) for r in res.results], axis=0)
